# version 30
# baseline (speedup 1.0000x reference)
import math
import numpy as np
import ml_dtypes
import concourse.bass as bass
import concourse.mybir as mybir
from concourse.bass_utils import run_bass_kernel_spmd
from contextlib import ExitStack

F32 = mybir.dt.float32
BF16 = mybir.dt.bfloat16
I32 = mybir.dt.int32
AF = mybir.ActivationFunctionType
ALU = mybir.AluOpType
AX = mybir.AxisListType
BFDT = ml_dtypes.bfloat16

C = 8
S = 8192
B = 2
H = 2048
NH = 16
OWN = 1024
XAW = 1030
NQ = 103
WR = 1030
T = 512
XYP = 524
EPS = 1e-5
ISQ = 1.0 / math.sqrt(128.0)

M_LO = [(1024 * c + 9) // 10 for c in range(C)]
XA0 = [10 * m for m in M_LO]
DELTA = [XA0[c] - 1024 * c for c in range(C)]
VL = [(XA0[c + 1] if c + 1 < C else 8200) - XA0[c] for c in range(C)]
GCH = [(0, 50), (50, 50), (100, 3)]
CB = [(0, 512), (512, 512), (1024, 512), (1536, 512), (2048, 12)]


def _asm_const(c, t0):
    d = DELTA[c]
    m0 = 0 if t0 == 0 else (t0 - d + 9) // 10
    o0 = 10 * m0 + d - t0
    return m0, o0


class Plan:
    def __init__(self, nc, stack):
        self.nc = nc
        self.q = {"tensor": [], "vector": [], "scalar": [], "sync": [], "gpsimd": []}
        self.sem = {}
        self.cnt = {}
        self.stack = stack
        self.log = {}

    def mksem(self, name):
        self.sem[name] = self.stack.enter_context(self.nc.semaphore(name))
        self.cnt[name] = 0

    def wait(self, eng, name, val):
        if val is None or val <= 0:
            return
        h = self.sem[name]
        self.q[eng].append(lambda e, h=h, v=val: e.wait_ge(h, v))
        self.log.setdefault(eng, []).append(("wait", name, val))

    def group(self, eng, body, inc=None, k=1, tag=""):
        val = None
        if inc is not None:
            self.cnt[inc] += k
            val = self.cnt[inc]
            h = self.sem[inc]

            def run(e, body=body, h=h, k=k):
                body(e).then_inc(h, k)
        else:
            def run(e, body=body):
                body(e)
        self.q[eng].append(run)
        self.log.setdefault(eng, []).append(("inc", inc, k, tag))
        return val

    def dma(self, eng, out, in_, semname):
        return self.group(eng, lambda e, o=out, i=in_: e.dma_start(out=o, in_=i),
                          inc=semname, k=16)


def build_program(taps=()):
    nc = bass.Bass("TRN2", target_bir_lowering=False)

    xa = nc.dram_tensor("xa", [B, H, XAW], F32, kind="ExternalInput").ap()
    xf = nc.dram_tensor("xf", [B, H, OWN], F32, kind="ExternalInput").ap()
    wq4 = nc.dram_tensor("wq4", [NH, 128, H], F32, kind="ExternalInput").ap()
    wk4 = nc.dram_tensor("wk4", [NH, 128, H], F32, kind="ExternalInput").ap()
    wv4 = nc.dram_tensor("wv4", [NH, 128, H], F32, kind="ExternalInput").ap()
    wo4 = nc.dram_tensor("wo4", [NH, 128, H], F32, kind="ExternalInput").ap()
    w0f = nc.dram_tensor("w0f", [8, 128, 2048], F32, kind="ExternalInput").ap()
    w1f = nc.dram_tensor("w1f", [8, 128, 2048], F32, kind="ExternalInput").ap()
    bq_d = nc.dram_tensor("bq", [128, 16], F32, kind="ExternalInput").ap()
    bk_d = nc.dram_tensor("bk", [128, 16], F32, kind="ExternalInput").ap()
    bv_d = nc.dram_tensor("bv", [128, 16], F32, kind="ExternalInput").ap()
    bo_d = nc.dram_tensor("bo", [128, 16], F32, kind="ExternalInput").ap()
    b0_d = nc.dram_tensor("b0", [128, 64], F32, kind="ExternalInput").ap()
    b1_d = nc.dram_tensor("b1", [128, 16], F32, kind="ExternalInput").ap()
    cid_d = nc.dram_tensor("cid", [1, 1], I32, kind="ExternalInput").ap()
    mask_d = nc.dram_tensor("mask", [1, XAW], BF16, kind="ExternalInput").ap()
    out_d = nc.dram_tensor("out", [B, H, OWN], F32, kind="ExternalOutput").ap()

    tap_d = {}
    for tname, tshape, tdt in taps:
        tap_d[tname] = nc.dram_tensor("tap_" + tname, tshape, tdt, kind="ExternalOutput").ap()

    agin = nc.dram_tensor("agin", [2048, 2048], BF16, kind="Internal").ap()
    watt = nc.dram_tensor("watt", [3, 16, 128, 2048], BF16, kind="Internal").ap()
    agout = nc.dram_tensor("agout", [16384, 2048], BF16, kind="Internal", addr_space="Shared").ap()
    vrin = nc.dram_tensor("vrin", [128, 32], F32, kind="Internal").ap()
    vrout = nc.dram_tensor("vrout", [128, 32], F32, kind="Internal").ap()

    VSUM_LD = 32

    with ExitStack() as stack:
        sb = lambda n_, s_, d_=F32: stack.enter_context(nc.sbuf_tensor(n_, s_, d_))
        P = Plan(nc, stack)

        bq_t = sb("bq_t", [128, 16]); bk_t = sb("bk_t", [128, 16])
        bv_t = sb("bv_t", [128, 16]); bo_t = sb("bo_t", [128, 16])
        b0_t = sb("b0_t", [128, 64]); b1_t = sb("b1_t", [128, 16])
        attn_sb = sb("attn_sb", [128, NH * 208])
        vsum = sb("vsum", [128, 32])
        ones_col_bf = sb("ones_col_bf", [128, 1], BF16)
        ones_row_f = sb("ones_row_f", [1, 128])
        eps_c = sb("eps_c", [1, 1])
        cid_s = sb("cid_s", [1, 1], I32)

        for s_ in ["dXA", "dWS0", "dWS1", "dSTG0", "dSTG1", "dSW0", "dSW1", "dWS2", "dWS3", "dBI", "dWC",
                   "dVR", "dXF0", "dXF1", "dOUT", "dFS0", "dFS1", "dFS2",
                   "sPE", "sVE", "sSE", "sCC", "dTAP"]:
            P.mksem(s_)

        pv = [stack.enter_context(nc.psum_tensor(f"pv{i}", [128, 512], F32)) for i in range(2)]
        pe = [stack.enter_context(nc.psum_tensor(f"pe{i}", [128, 512], F32)) for i in range(2)]
        pb = [stack.enter_context(nc.psum_tensor(f"pb{i}", [128, 512], F32)) for i in range(2)]
        pst = [stack.enter_context(nc.psum_tensor(f"pst{i}", [1, 512], F32)) for i in range(2)]
        free = {}

        def guard(engq, p):
            f = free.get(id(p))
            if f is not None:
                P.wait(engq, f[0], f[1])

        def release(p, semname, val):
            free[id(p)] = (semname, val)

        RH = {}

        att = ExitStack()
        sba = lambda n_, s_, d_=F32: att.enter_context(nc.sbuf_tensor(n_, s_, d_))
        ones_row_bf = sba("ones_row_bf", [1, 128], BF16)
        mask_s = sba("mask_s", [1, XAW], BF16)
        x_bf = sba("x_bf", [128, NH * 2 * XAW], BF16)
        v_bf = sba("v_bf", [128, NH * 2 * XAW], BF16)
        xs_bf = sba("xs_bf", [128, NH * 206], BF16)
        q_all = sba("q_all", [128, NH * 206], BF16)
        ctx_bf = sba("ctx_bf", [128, NH * 208], BF16)
        kh_bf = sba("kh_bf", [128, 2 * XAW], BF16)
        wsl = sba("wsl", [128, 3 * 2048], BF16)
        w_sb = sba("w_sb", [1, 4 * XAW], BF16)
        prod_bf = sba("prod_bf", [128, 2 * XAW], BF16)
        vw_f = sba("vw_f", [128, 512], BF16)
        winsum = sba("winsum", [128, 104])
        zall = sba("zall", [1, NH * 2 * 104])
        vpart = sba("vpart", [128, 32])

        def wslb(i):
            return wslb_slot(SLOT_OF[i])

        P.dma("gpsimd", cid_s[:], cid_d, "dXA")
        P.dma("gpsimd", mask_s[:], mask_d, "dXA")
        for fc in range(NH):
            P.dma("gpsimd", x_bf[:, fc * 2 * XAW: fc * 2 * XAW + XAW],
                  xa[0, fc * 128:(fc + 1) * 128, :], "dXA")
        XA_B0 = P.cnt["dXA"]
        XA_ALL = XA_B0 + 16 * NH
        for dst, src in [(bq_t, bq_d), (bk_t, bk_d), (bv_t, bv_d), (bo_t, bo_d),
                         (b0_t, b0_d), (b1_t, b1_d)]:
            P.dma("sync", dst[:], src, "dBI")
        P.wait("scalar", "dBI", P.cnt["dBI"])
        P.wait("vector", "dBI", P.cnt["dBI"])

        P.wait("vector", "dXA", 32)

        def prelim(v):
            v.memset(ones_col_bf[:], 1.0)
            v.memset(ones_row_bf[:], 1.0)
            v.memset(ones_row_f[:], 1.0)
            v.memset(eps_c[:], EPS)
            idx = v.value_load(cid_s[:])
            RH["cid"] = v.compute_val(idx)
            return v.memset(zall[:, 0:104], 0.0)
        P.group("vector", prelim, inc="sVE")

        P.wait("vector", "dXA", XA_ALL)

        def build_xs(v):
            last = None
            for fc in range(NH):
                for b in range(B):
                    last = v.tensor_copy(
                        out=xs_bf[:, fc * 206 + b * NQ: fc * 206 + b * NQ + NQ],
                        in_=x_bf[:, fc * 2 * XAW + b * XAW: fc * 2 * XAW + b * XAW + WR: 10])
            return last
        xs_done = P.group("vector", build_xs, inc="sVE")

        vgrp = {}
        WSEQ = {"n": 0}
        wsl_cons = {}
        DWS = ["dWS0", "dWS1", "dWS2", "dWS3"]
        SLOT_OF = {}
        slot_prev = {}
        XS_FREE = {}

        def slot_of(si):
            if si < 32:
                return si % 3
            return [3, 0, 1, 2][(si - 32) % 4]

        def wslb_slot(sl):
            if sl == 3:
                return xs_bf[:, 0:2048]
            return wsl[:, sl * 2048: sl * 2048 + 2048]

        def wsl_load(queue, src):
            si = WSEQ["n"]; WSEQ["n"] += 1
            sl = slot_of(si)
            SLOT_OF[si] = sl
            prev = slot_prev.get(sl)
            if prev is not None:
                P.wait(queue, "sPE", wsl_cons[prev])
            if sl == 3 and prev is None:
                P.wait(queue, "sPE", XS_FREE["q"])
            slot_prev[sl] = si
            d = P.dma(queue, wslb_slot(sl), src, DWS[sl])
            return si, d

        wv_done = {}

        def load_wv(dc):
            wv_done[dc] = wsl_load("gpsimd", wv4[dc])

        load_wv(0)
        load_wv(1)
        for fc in range(NH):
            P.dma("gpsimd", x_bf[:, fc * 2 * XAW + XAW: fc * 2 * XAW + 2 * XAW],
                  xa[1, fc * 128:(fc + 1) * 128, :], "dXA")
        assert P.cnt["dXA"] == XA_ALL
        WC_DONE = {}
        WCD = {}
        WC_PIECE = {}
        for pc in range(4):
            WC_PIECE[(2 + pc, )] = [(0, pc)]

        agv = agin.rearrange("(p s) c -> p s c", s=16)

        def agcast(sl):
            src = w0f[sl] if sl < 8 else w1f[sl - 8]
            P.dma("gpsimd", agv[:, sl, :], src, "dSW0")

        for dc in range(NH):
            vsi, vd = wv_done[dc]
            P.wait("tensor", DWS[SLOT_OF[vsi]], vd)
            for cb, (c0, cw) in enumerate(CB):
                if dc == 0:
                    P.wait("tensor", "dXA", XA_B0 if cb < 2 else XA_ALL)
                p = pv[(dc * 5 + cb) % 2]
                guard("tensor", p)

                def vmm(t, vsi=vsi, c0=c0, cw=cw, p=p):
                    last = None
                    for fc in range(NH):
                        last = t.matmul(p[:, 0:cw], wslb(vsi)[:, fc * 128:(fc + 1) * 128],
                                        x_bf[:, fc * 2 * XAW + c0: fc * 2 * XAW + c0 + cw],
                                        start=(fc == 0), stop=(fc == NH - 1))
                    return last
                g = P.group("tensor", vmm, inc="sPE")
                vgrp[(dc, cb)] = g
                P.wait("scalar", "sPE", g)
                ev = P.group("scalar",
                             lambda s_, dc=dc, c0=c0, cw=cw, p=p: s_.activation(
                                 out=v_bf[:, dc * 2 * XAW + c0: dc * 2 * XAW + c0 + cw],
                                 in_=p[:, 0:cw], func=AF.Identity, bias=bv_t[:, dc:dc + 1], scale=1.0),
                             inc="sSE")
                release(p, "sSE", ev)
            wsl_cons[wv_done[dc][0]] = P.cnt["sPE"]
            if dc + 2 < NH:
                load_wv(dc + 2)
                agcast(dc)
            for (i, pc) in WC_PIECE.get((dc,), []):
                P.wait("gpsimd", "sPE", vgrp[(dc, 4)])
                WC_DONE[i] = P.dma("gpsimd", watt[i, pc * 4:(pc + 1) * 4],
                                   [wq4, wk4][i][pc * 4:(pc + 1) * 4], "dWC")
                WCD[(i, pc)] = WC_DONE[i]
        V_EVAC_ALL = P.cnt["sSE"]
        agcast(14)
        agcast(15)
        P.wait("gpsimd", "dSW0", P.cnt["dSW0"])
        ag_done = P.group("gpsimd", lambda g: g.collective_compute(
            "AllGather", ALU.bypass, replica_groups=[list(range(C))],
            ins=[agin.opt()], outs=[agout.opt()]), inc="sCC")

        if "v_bf" in tap_d:
            P.wait("sync", "sSE", V_EVAC_ALL)
            P.dma("sync", tap_d["v_bf"], v_bf[:], "dTAP")

        qgrp = {}
        wq_done = {}
        for dc in range(2):
            if dc == 0:
                P.wait("sync", "dWC", WC_DONE[0])
            wq_done[dc] = wsl_load("sync", watt[0, dc])
        for dc in range(NH):
            qsi, qd = wq_done[dc]
            P.wait("tensor", DWS[SLOT_OF[qsi]], qd)
            if dc == 0:
                P.wait("tensor", "sVE", xs_done)
            p = pv[dc % 2]
            guard("tensor", p)

            def qmm(t, qsi=qsi, p=p):
                last = None
                for fc in range(NH):
                    last = t.matmul(p[:, 0:206], wslb(qsi)[:, fc * 128:(fc + 1) * 128],
                                    xs_bf[:, fc * 206:(fc + 1) * 206],
                                    start=(fc == 0), stop=(fc == NH - 1))
                return last
            g = P.group("tensor", qmm, inc="sPE")
            qgrp[dc] = g
            wsl_cons[qsi] = g
            if dc + 2 < NH:
                wq_done[dc + 2] = wsl_load("sync", watt[0, dc + 2])
            if dc == NH - 1:
                XS_FREE["q"] = qgrp[NH - 1]
            P.wait("scalar", "sPE", g)
            ev = P.group("scalar", lambda s_, dc=dc, p=p: s_.activation(
                out=q_all[:, dc * 206:(dc + 1) * 206], in_=p[:, 0:206],
                func=AF.Identity, bias=bq_t[:, dc:dc + 1], scale=1.0), inc="sSE")
            release(p, "sSE", ev)

        P.wait("vector", "sSE", V_EVAC_ALL)

        def vsum_switch(v):
            for ci in nc.Switch(engines=[nc.vector], index=[RH["cid"]], n=C):
                ln = VL[ci]
                for dc in range(NH):
                    for b in range(B):
                        v.tensor_reduce(
                            out=vpart[:, dc * 2 + b: dc * 2 + b + 1],
                            in_=v_bf[:, dc * 2 * XAW + b * XAW: dc * 2 * XAW + b * XAW + ln],
                            axis=AX.X, op=ALU.add)
            return v.scalar_tensor_tensor(
                out=vpart[:].rearrange("p (h b) -> p h b", b=2),
                in0=bv_t[:].unsqueeze(2).broadcast_to([128, 16, 2]),
                scalar=-1.0,
                in1=vpart[:].rearrange("p (h b) -> p h b", b=2),
                op0=ALU.mult, op1=ALU.add)
        vpart_done = P.group("vector", vsum_switch, inc="sVE")

        kgrp = {}
        st_prod = {}
        st_scores = {}
        st_wchain = {}
        st_wbp = {}
        st_ctx = {}

        wk_done = {}

        def load_wk(h):
            wk_done[h] = wsl_load("gpsimd", wk4[h])

        def emit_k(h, cbs=None):
            ksi, kd = wk_done[h]
            P.wait("tensor", DWS[SLOT_OF[ksi]], kd)
            if h >= 1 and (cbs is None or cbs[0] == 0):
                P.wait("scalar", "sVE", st_prod[h - 1])
            for cb, (c0, cw) in enumerate(CB):
                if cbs is not None and cb not in cbs:
                    continue
                p = pv[cb % 2]
                guard("tensor", p)

                def kmm(t, ksi=ksi, c0=c0, cw=cw, p=p):
                    last = None
                    for fc in range(NH):
                        last = t.matmul(p[:, 0:cw], wslb(ksi)[:, fc * 128:(fc + 1) * 128],
                                        x_bf[:, fc * 2 * XAW + c0: fc * 2 * XAW + c0 + cw],
                                        start=(fc == 0), stop=(fc == NH - 1))
                    return last
                g = P.group("tensor", kmm, inc="sPE")
                P.wait("scalar", "sPE", g)
                ev = P.group("scalar", lambda s_, c0=c0, cw=cw, p=p, h=h: s_.activation(
                    out=kh_bf[:, c0:c0 + cw], in_=p[:, 0:cw], func=AF.Identity,
                    bias=bk_t[:, h:h + 1], scale=1.0), inc="sSE")
                release(p, "sSE", ev)
            kgrp[h] = P.cnt["sPE"]
            wsl_cons[wk_done[h][0]] = P.cnt["sPE"]
            return P.cnt["sSE"]

        def emit_prod(h, kev):
            P.wait("vector", "sSE", kev)
            if h >= 1:
                P.wait("vector", "sPE", st_scores[h - 1])

            def pmul(v, h=h):
                kv = kh_bf[:].rearrange("p (m t) -> p m t", t=10)
                qv = q_all[:, h * 206:(h + 1) * 206].unsqueeze(2).broadcast_to([128, 206, 10])
                ov = prod_bf[:].rearrange("p (m t) -> p m t", t=10)
                return v.tensor_tensor(out=ov, in0=kv, in1=qv, op=ALU.mult)
            st_prod[h] = P.group("vector", pmul, inc="sVE")

        def wslot(h, b):
            j = (2 * h + b) % 4
            return w_sb[0:1, j * XAW: j * XAW + XAW]

        def emit_scores(h):
            for b in range(B):
                wr = wslot(h, b)
                for cix, (g0, ng) in enumerate(GCH):
                    par = (b * 3 + cix) % 2
                    w = 10 * ng
                    p = pst[par]
                    if cix == 0 and b == 0:
                        P.wait("tensor", "sVE", st_prod[h])
                    guard("tensor", p)
                    g = P.group("tensor", lambda t, b=b, g0=g0, w=w, p=p: t.matmul(
                        p[:, 0:w], ones_col_bf[:],
                        prod_bf[:, b * XAW + 10 * g0: b * XAW + 10 * g0 + w],
                        start=True, stop=True), inc="sPE")
                    st_scores[h] = g
                    P.wait("scalar", "sPE", g)
                    if cix == 0 and h >= 2:
                        P.wait("scalar", "sPE", st_wbp.get((h - 2, b), 0))
                    ev = P.group("scalar", lambda s_, g0=g0, w=w, p=p, wr=wr: s_.activation(
                        out=wr[0:1, 10 * g0: 10 * g0 + w],
                        in_=p[:, 0:w], func=AF.Exp, scale=ISQ), inc="sSE")
                    release(p, "sSE", ev)
                exp_done = P.cnt["sSE"]
                P.wait("vector", "sSE", exp_done)
                zsl = zall[0:1, (h * 2 + b) * 104: (h * 2 + b) * 104 + NQ]

                def wchain(v, wr=wr, zsl=zsl):
                    v.scalar_tensor_tensor(out=wr[0:1, 0:WR], in0=wr[0:1, 0:WR],
                                           scalar=-1.0, in1=mask_s[:, 0:WR],
                                           op0=ALU.add, op1=ALU.mult)
                    v.tensor_reduce(out=zsl,
                                    in_=wr[0:1, 0:WR].rearrange("p (m t) -> p m t", t=10),
                                    axis=AX.X, op=ALU.add)
                    v.tensor_scalar(out=zsl, in0=zsl,
                                    scalar1=float(S), scalar2=None, op0=ALU.add)
                    return v.reciprocal(out=zsl, in_=zsl)
                st_wchain[(h, b)] = P.group("vector", wchain, inc="sVE")

        def emit_wb(h, b, wrow_src):
            for cix, (g0, ng) in enumerate(GCH):
                par = (b * 3 + cix) % 2
                w = 10 * ng
                p = pb[par]
                P.wait("tensor", "sVE", st_wchain[(h, b)])
                guard("tensor", p)
                g = P.group("tensor", lambda t, g0=g0, w=w, p=p, wrow_src=wrow_src: t.matmul(
                    p[:, 0:w], ones_row_bf[:], wrow_src[0:1, 10 * g0: 10 * g0 + w],
                    start=True, stop=True), inc="sPE")
                P.wait("vector", "sPE", g)

                def vwin(v, h=h, b=b, g0=g0, ng=ng, w=w, p=p):
                    v.tensor_tensor(out=vw_f[:, 0:w],
                                    in0=v_bf[:, h * 2 * XAW + b * XAW + 10 * g0:
                                             h * 2 * XAW + b * XAW + 10 * g0 + w],
                                    in1=p[:, 0:w], op=ALU.mult)
                    return v.tensor_reduce(out=winsum[:, g0:g0 + ng],
                                           in_=vw_f[:, 0:w].rearrange("p (m t) -> p m t", t=10),
                                           axis=AX.X, op=ALU.add)
                vv = P.group("vector", vwin, inc="sVE")
                release(p, "sVE", vv)
            zsl = zall[0:1, (h * 2 + b) * 104: (h * 2 + b) * 104 + NQ]
            prz = pb[(b * 3 + 3) % 2]
            guard("tensor", prz)
            g = P.group("tensor", lambda t, prz=prz, zsl=zsl: t.matmul(
                prz[:, 0:NQ], ones_row_f[:], zsl, start=True, stop=True), inc="sPE")
            st_wbp[(h, b)] = g
            P.wait("vector", "sPE", g)

            def ctxop(v, h=h, b=b, prz=prz):
                return v.tensor_tensor(out=ctx_bf[:, h * 208 + b * NQ: h * 208 + (b + 1) * NQ],
                                       in0=winsum[:, 0:NQ], in1=prz[:, 0:NQ], op=ALU.mult)
            cv = P.group("vector", ctxop, inc="sVE")
            release(prz, "sVE", cv)
            st_ctx[h] = cv

        load_wk(0)
        load_wk(1)
        P.wait("gpsimd", "sVE", vpart_done)
        P.dma("gpsimd", vrin, vpart[:], "dVR")
        P.wait("gpsimd", "dVR", 16)
        ar_done = P.group("gpsimd", lambda g: g.collective_compute(
            "AllReduce", ALU.add, replica_groups=[list(range(C))],
            ins=[vrin.opt()], outs=[vrout.opt()]), inc="sCC")
        P.wait("gpsimd", "sCC", ar_done)
        vsum_ld = P.dma("gpsimd", vsum[:], vrout, "dVR")
        assert vsum_ld == VSUM_LD, vsum_ld

        for h in range(NH + 2):
            if h < NH:
                emit_k(h, cbs=(0, 1))
            if h >= 2:
                emit_wb(h - 2, 0, wslot(h - 2, 0))
                emit_wb(h - 2, 1, wslot(h - 2, 1))
            if h < NH:
                kev = emit_k(h, cbs=(2, 3, 4))
            if 1 <= h <= NH:
                emit_scores(h - 1)
            if h < NH:
                emit_prod(h, kev)
            if h + 2 < NH:
                load_wk(h + 2)

        XF0_PE = max(kgrp[NH - 1], st_wbp[(NH - 1, 0)], st_wbp[(NH - 1, 1)])
        XF0_VE = st_wchain[(NH - 1, 1)]

        P.wait("tensor", "sVE", st_ctx[NH - 1])
        P.wait("vector", "dVR", VSUM_LD)
        for h in range(NH):
            for b in range(B):
                zsl = zall[0:1, (h * 2 + b) * 104: (h * 2 + b) * 104 + NQ]
                prz = pb[(2 * h + b) % 2]
                guard("tensor", prz)
                g = P.group("tensor", lambda t, prz=prz, zsl=zsl: t.matmul(
                    prz[:, 0:NQ], ones_row_f[:], zsl, start=True, stop=True), inc="sPE")
                P.wait("vector", "sPE", g)

                def ctxadd(v, h=h, b=b, prz=prz):
                    return v.scalar_tensor_tensor(
                        out=ctx_bf[:, h * 208 + b * NQ: h * 208 + (b + 1) * NQ],
                        in0=prz[:, 0:NQ],
                        scalar=vsum[:, h * 2 + b: h * 2 + b + 1],
                        in1=ctx_bf[:, h * 208 + b * NQ: h * 208 + (b + 1) * NQ],
                        op0=ALU.mult, op1=ALU.add)
                cv = P.group("vector", ctxadd, inc="sVE")
                release(prz, "sVE", cv)
        ctx_final = P.cnt["sVE"]

        P.wait("vector", "dVR", VSUM_LD)
        ctxu_done = P.group("vector", lambda v: v.tensor_scalar(
            out=ctx_bf[:].rearrange("p (h c) -> p h c", c=208)[:, :, 206:208],
            in0=vsum[:].rearrange("p (h b) -> p h b", b=2),
            scalar1=1.0 / S, scalar2=None, op0=ALU.mult), inc="sVE")

        ogrp = {}
        wo_done = {}
        for dc in range(2):
            wo_done[dc] = wsl_load("gpsimd", wo4[dc])
        for dc in range(NH):
            osi, od = wo_done[dc]
            P.wait("tensor", DWS[SLOT_OF[osi]], od)
            if dc == 0:
                P.wait("tensor", "sVE", ctx_final)
                P.wait("tensor", "sVE", ctxu_done)
            p = pv[dc % 2]
            guard("tensor", p)

            def omm(t, osi=osi, p=p):
                last = None
                for hc in range(NH):
                    last = t.matmul(p[:, 0:208], wslb(osi)[:, hc * 128:(hc + 1) * 128],
                                    ctx_bf[:, hc * 208:(hc + 1) * 208],
                                    start=(hc == 0), stop=(hc == NH - 1))
                return last
            g = P.group("tensor", omm, inc="sPE")
            ogrp[dc] = g
            wsl_cons[osi] = g
            if dc + 2 < NH:
                wo_done[dc + 2] = wsl_load("gpsimd", wo4[dc + 2])
            P.wait("scalar", "sPE", g)
            ev = P.group("scalar", lambda s_, dc=dc, p=p: s_.activation(
                out=attn_sb[:, dc * 208:(dc + 1) * 208], in_=p[:, 0:208],
                func=AF.Identity, bias=bo_t[:, dc:dc + 1], scale=1.0), inc="sSE")
            release(p, "sSE", ev)
        OP_EVAC_ALL = P.cnt["sSE"]

        P.wait("vector", "sSE", OP_EVAC_ALL)

        def attnprime(v):
            last = None
            for dc in range(NH):
                for b in range(B):
                    last = v.tensor_scalar(
                        out=attn_sb[:, dc * 208 + b * NQ: dc * 208 + (b + 1) * NQ],
                        in0=attn_sb[:, dc * 208 + b * NQ: dc * 208 + (b + 1) * NQ],
                        scalar1=attn_sb[:, dc * 208 + 206 + b: dc * 208 + 207 + b],
                        scalar2=None, op0=ALU.subtract)
            return last
        attnp_done = P.group("vector", attnprime, inc="sVE")

        if "attn_sb" in tap_d:
            P.wait("sync", "sVE", attnp_done)
            P.dma("sync", tap_d["attn_sb"], attn_sb[:], "dTAP")
        if "ctx_bf" in tap_d:
            P.wait("sync", "sVE", st_ctx[NH - 1])
            P.wait("sync", "sVE", ctxu_done)
            P.dma("sync", tap_d["ctx_bf"], ctx_bf[:], "dTAP")
        if "vsum" in tap_d:
            P.wait("sync", "dVR", VSUM_LD)
            P.dma("sync", tap_d["vsum"], vsum[:], "dTAP")
        if "q_all" in tap_d:
            P.wait("sync", "sVE", attnp_done)
            P.dma("sync", tap_d["q_all"], q_all[:], "dTAP")

        ATT_PE = P.cnt["sPE"]
        ATT_VE = P.cnt["sVE"]
        ATT_SE = P.cnt["sSE"]
        att.close()

        ffn = ExitStack()
        sbf = lambda n_, s_, d_=F32: ffn.enter_context(nc.sbuf_tensor(n_, s_, d_))
        xy = [sbf(f"xy{i}", [128, NH * XYP]) for i in range(2)]
        ybf = [sbf(f"ybf{i}", [128, NH * T], BF16) for i in range(2)]
        dbf = sbf("dbf", [128, 64 * T], BF16)
        fsl = sbf("fsl", [128, 3 * 4352], BF16)
        sqt = sbf("sqt", [128, 2 * 512], BF16)
        mrow = sbf("mrow", [1, 512])
        vrow = sbf("vrow", [1, 512])

        def fslb(i):
            return fsl[:, (i % 3) * 4352:(i % 3) * 4352 + 4096]

        def agpair(r, sl):
            blk = agout[2048 * r: 2048 * (r + 1), :].rearrange("(p s) c -> p s c", s=16)
            return blk[:, sl:sl + 2, :].rearrange("p s c -> p (s c)")

        FS = ["dFS0", "dFS1", "dFS2"]
        pair_consumed = {}
        gpair = [0]

        def ln_stats(p_ix, sq_guard):
            xyt = xy[p_ix]
            yb = ybf[p_ix]

            guard("tensor", pst[0])
            sq_marks = []
            cast_done = None
            for fc in range(NH):
                if fc >= 2:
                    P.wait("vector", "sPE", sq_marks[fc - 2])
                elif fc == 0 and sq_guard:
                    P.wait("vector", "sPE", sq_guard)

                def castsq(v, fc=fc):
                    v.tensor_copy(out=yb[:, fc * T:(fc + 1) * T],
                                  in_=xyt[:, fc * XYP: fc * XYP + T])
                    return v.tensor_tensor(out=sqt[:, (fc % 2) * 512:(fc % 2) * 512 + T],
                                           in0=yb[:, fc * T:(fc + 1) * T],
                                           in1=yb[:, fc * T:(fc + 1) * T], op=ALU.mult)
                sqv = P.group("vector", castsq, inc="sVE")
                P.wait("tensor", "sVE", sqv)
                if fc == 0:
                    guard("tensor", pst[1])

                def sm2(t, fc=fc):
                    t.matmul(pst[0][:, 0:T], ones_col_bf[:], yb[:, fc * T:(fc + 1) * T],
                             start=(fc == 0), stop=(fc == NH - 1))
                    return t.matmul(pst[1][:, 0:T], ones_col_bf[:],
                                    sqt[:, (fc % 2) * 512:(fc % 2) * 512 + T],
                                    start=(fc == 0), stop=(fc == NH - 1))
                sq_marks.append(P.group("tensor", sm2, inc="sPE"))
            cast_done = sq_marks[-1]
            sumsq_done = sq_marks[-1]
            P.wait("vector", "sPE", sumsq_done)

            def rows(v):
                v.tensor_scalar(out=mrow[:, 0:T], in0=pst[0][:, 0:T], scalar1=1.0 / H,
                                scalar2=None, op0=ALU.mult)
                v.tensor_scalar(out=vrow[:, 0:T], in0=pst[1][:, 0:T], scalar1=1.0 / H,
                                scalar2=None, op0=ALU.mult)
                v.tensor_tensor(out=pst[0][:, 0:T], in0=mrow[:, 0:T], in1=mrow[:, 0:T], op=ALU.mult)
                return v.tensor_tensor(out=vrow[:, 0:T], in0=vrow[:, 0:T], in1=pst[0][:, 0:T],
                                       op=ALU.subtract)
            rows_done = P.group("vector", rows, inc="sVE")
            release(pst[0], "sVE", rows_done)
            release(pst[1], "sVE", rows_done)
            P.wait("scalar", "sVE", rows_done)

            def rstd(s_):
                s_.activation(out=vrow[:, 0:T], in_=vrow[:, 0:T], func=AF.Ln, bias=eps_c[0:1, 0:1])
                return s_.activation(out=vrow[:, 0:T], in_=vrow[:, 0:T], func=AF.Exp, scale=-0.5)
            rstd_done = P.group("scalar", rstd, inc="sSE")
            P.wait("tensor", "sVE", rows_done)
            P.wait("tensor", "sSE", rstd_done)
            guard("tensor", pb[0]); guard("tensor", pb[1])
            P.group("tensor", lambda t: t.matmul(pb[0][:, 0:T], ones_row_f[:],
                                                 mrow[0:1, 0:T], start=True, stop=True), inc="sPE")
            bt = P.group("tensor", lambda t: t.matmul(pb[1][:, 0:T], ones_row_f[:],
                                                      vrow[0:1, 0:T], start=True, stop=True), inc="sPE")
            return bt, sumsq_done

        def ln_apply(p_ix, bt):
            xyt = xy[p_ix]
            P.wait("vector", "sPE", bt)
            marks = []
            for fc0 in range(0, NH, 4):

                def apply_(v, fc0=fc0):
                    last = None
                    for fc in range(fc0, fc0 + 4):
                        v.tensor_tensor(out=xyt[:, fc * XYP: fc * XYP + T],
                                        in0=xyt[:, fc * XYP: fc * XYP + T], in1=pb[0][:, 0:T],
                                        op=ALU.subtract)
                        last = v.tensor_tensor(out=xyt[:, fc * XYP: fc * XYP + T],
                                               in0=xyt[:, fc * XYP: fc * XYP + T], in1=pb[1][:, 0:T],
                                               op=ALU.mult)
                    return last
                marks.append(P.group("vector", apply_, inc="sVE"))
            release(pb[0], "sVE", marks[-1])
            release(pb[1], "sVE", marks[-1])
            return marks

        TILES = [(0, 0), (0, T), (1, 0), (1, T)]
        last_ssq = [0]
        XF_LD = {}

        P.wait("sync", "sVE", XF0_VE)
        P.wait("sync", "sPE", XF0_PE)
        for fc in range(NH):
            P.dma("sync", xy[0][:, fc * XYP: fc * XYP + T],
                  xf[0, fc * 128:(fc + 1) * 128, 0:T], "dXF0")
        XF_LD[0] = P.cnt["dXF0"]

        def prep(t_ix):
            b, t0 = TILES[t_ix]
            p_ix = t_ix % 2
            dxf = "dXF0" if p_ix == 0 else "dXF1"
            if t_ix not in XF_LD:
                if t_ix >= 2:
                    P.wait("gpsimd", "dOUT", 256 * (t_ix - 1))
                for fc in range(NH):
                    P.dma("gpsimd", xy[p_ix][:, fc * XYP: fc * XYP + T],
                          xf[b, fc * 128:(fc + 1) * 128, t0:t0 + T], dxf)
                XF_LD[t_ix] = P.cnt[dxf]
            P.wait("vector", dxf, XF_LD[t_ix])
            P.wait("vector", "sVE", attnp_done)

            def uadd(v, b=b, p_ix=p_ix):
                last = None
                for fc in range(NH):
                    last = v.tensor_scalar(
                        out=xy[p_ix][:, fc * XYP: fc * XYP + T],
                        in0=xy[p_ix][:, fc * XYP: fc * XYP + T],
                        scalar1=attn_sb[:, fc * 208 + 206 + b: fc * 208 + 207 + b],
                        scalar2=None, op0=ALU.add)
                return last
            P.group("vector", uadd, inc="sVE")

            def sw_asm(v, b=b, t0=t0, p_ix=p_ix):
                for ci in nc.Switch(engines=[nc.vector], index=[RH["cid"]], n=C):
                    m0, o0 = _asm_const(ci, t0)
                    for fc in range(NH):
                        v.tensor_tensor(
                            out=xy[p_ix][:, fc * XYP + o0: fc * XYP + o0 + 511: 10],
                            in0=xy[p_ix][:, fc * XYP + o0: fc * XYP + o0 + 511: 10],
                            in1=attn_sb[:, fc * 208 + b * NQ + m0: fc * 208 + b * NQ + m0 + 52],
                            op=ALU.add)
                return v.tensor_copy(out=xy[p_ix][:, 0:1], in_=xy[p_ix][:, 0:1])
            P.group("vector", sw_asm, inc="sVE")

        def ln1_block(t_ix):
            p_ix = t_ix % 2
            bt, ssq = ln_stats(p_ix, last_ssq[0])
            last_ssq[0] = ssq
            ln_apply(p_ix, bt)

            def recast(v, p_ix=p_ix):
                last = None
                for fc in range(NH):
                    last = v.tensor_copy(out=ybf[p_ix][:, fc * T:(fc + 1) * T],
                                         in_=xy[p_ix][:, fc * XYP: fc * XYP + T])
                return last
            return P.group("vector", recast, inc="sVE")

        def ln2_block(t_ix):
            b, t0 = TILES[t_ix]
            p_ix = t_ix % 2
            bt, ssq = ln_stats(p_ix, last_ssq[0])
            last_ssq[0] = ssq
            aps = ln_apply(p_ix, bt)
            for i, fc0 in enumerate(range(0, NH, 4)):
                q = ("sync" if (t_ix == 3 and i % 2 == 1) else "gpsimd")
                P.wait(q, "sVE", aps[i])
                for fc in range(fc0, fc0 + 4):
                    P.dma(q, out_d[b, fc * 128:(fc + 1) * 128, t0:t0 + T],
                          xy[p_ix][:, fc * XYP: fc * XYP + T], "dOUT")

        def load_pair(r, sl):
            pi = gpair[0]; gpair[0] += 1
            fs = FS[pi % 3]
            if pi < 3:
                P.wait("sync", "sCC", ag_done)
                if pi == 0:
                    P.wait("sync", "sSE", ATT_SE)
                    P.wait("sync", "sVE", ATT_VE)
                    P.wait("sync", "sPE", ATT_PE)
            else:
                P.wait("sync", "sPE", pair_consumed[pi - 3])
            sd = P.dma("sync", fslb(pi), agpair(r, sl), fs)
            P.wait("tensor", fs, sd)
            return pi

        def f1(t_ix, y_ready, hook=None, hook2=None):
            p_ix = t_ix % 2
            P.wait("tensor", "sVE", y_ready)
            for pr in range(32):
                if pr == 4 and hook is not None:
                    hook()
                if pr == 16 and hook2 is not None:
                    hook2()
                r, sl = pr // 4, (pr % 4) * 2
                pi = load_pair(r, sl)
                for half in range(2):
                    ncg = 2 * pr + half
                    p = pv[ncg % 2]
                    guard("tensor", p)

                    def dmm(t, pi=pi, half=half, p=p, p_ix=p_ix):
                        last = None
                        for fc in range(NH):
                            last = t.matmul(
                                p[:, 0:T],
                                fslb(pi)[:, half * 2048 + fc * 128: half * 2048 + (fc + 1) * 128],
                                ybf[p_ix][:, fc * T:(fc + 1) * T],
                                start=(fc == 0), stop=(fc == NH - 1))
                        return last
                    g = P.group("tensor", dmm, inc="sPE")
                    pair_consumed[pi] = g
                    P.wait("scalar", "sPE", g)
                    ev = P.group("scalar", lambda s_, ncg=ncg, p=p: s_.activation(
                        out=dbf[:, ncg * T:(ncg + 1) * T], in_=p[:, 0:T], func=AF.Gelu,
                        bias=b0_t[:, ncg:ncg + 1], scale=1.0), inc="sSE")
                    release(p, "sSE", ev)

        def f2(t_ix):
            p_ix = t_ix % 2
            ea_marks = []
            for hc in range(NH):
                pE = pe[hc % 2]
                guard("tensor", pE)
                g = None
                for subpair in range(2):
                    r, sl = hc // 2, 8 + (hc % 2) * 4 + 2 * subpair
                    pi = load_pair(r, sl)
                    for half in range(2):
                        sub = 2 * subpair + half

                        def emm(t, pi=pi, half=half, sub=sub, pE=pE):
                            last = None
                            for j in range(NH):
                                last = t.matmul(
                                    pE[:, 0:T],
                                    fslb(pi)[:, half * 2048 + j * 128: half * 2048 + (j + 1) * 128],
                                    dbf[:, (sub * 16 + j) * T:(sub * 16 + j + 1) * T],
                                    start=(sub == 0 and j == 0),
                                    stop=(sub == 3 and j == NH - 1))
                            return last
                        g = P.group("tensor", emm, inc="sPE")
                        pair_consumed[pi] = g
                P.wait("vector", "sPE", g)
                ea = P.group("vector", lambda v, hc=hc, p_ix=p_ix, pE=pE: v.scalar_tensor_tensor(
                    out=xy[p_ix][:, hc * XYP: hc * XYP + T],
                    in0=pE[:, 0:T], scalar=b1_t[:, hc:hc + 1],
                    in1=xy[p_ix][:, hc * XYP: hc * XYP + T],
                    op0=ALU.add, op1=ALU.add), inc="sVE")
                release(pE, "sVE", ea)
                ea_marks.append(ea)

        prep(0)
        y_ready = ln1_block(0)
        for t_ix in range(4):
            hook = (lambda t=t_ix: ln2_block(t - 1)) if t_ix >= 1 else None
            hook2 = (lambda t=t_ix: prep(t + 1)) if t_ix + 1 < 4 else None
            f1(t_ix, y_ready, hook=hook, hook2=hook2)
            if t_ix + 1 < 4:
                y_ready = ln1_block(t_ix + 1)
            f2(t_ix)
        ln2_block(3)

        P.wait("sync", "dOUT", P.cnt["dOUT"])
        if tap_d:
            P.wait("sync", "dTAP", P.cnt["dTAP"])

        blk = stack.enter_context(nc.Block())

        @blk.gpsimd
        def _(g):
            for fn in P.q["gpsimd"]:
                fn(g)

        @blk.vector
        def _(v):
            for fn in P.q["vector"]:
                fn(v)

        @blk.scalar
        def _(s_):
            for fn in P.q["scalar"]:
                fn(s_)

        @blk.tensor
        def _(t):
            for fn in P.q["tensor"]:
                fn(t)

        @blk.sync
        def _(sy):
            for fn in P.q["sync"]:
                fn(sy)

        ffn.close()
    global _LAST_PLAN_LOG
    _LAST_PLAN_LOG = P.log
    return nc


_LAST_PLAN_LOG = None


def _prep_inputs(inputs):
    x = np.asarray(inputs["L0_attn_hidden_states"], dtype=np.float32)
    wq = np.asarray(inputs["L0_attn_wq"], dtype=np.float32)
    bq = np.asarray(inputs["L0_attn_bq"], dtype=np.float32)
    wk = np.asarray(inputs["L0_attn_wk"], dtype=np.float32)
    bk = np.asarray(inputs["L0_attn_bk"], dtype=np.float32)
    wv = np.asarray(inputs["L0_attn_wv"], dtype=np.float32)
    bv = np.asarray(inputs["L0_attn_bv"], dtype=np.float32)
    wo = np.asarray(inputs["L0_attn_wo"], dtype=np.float32)
    bo = np.asarray(inputs["L0_attn_bo"], dtype=np.float32)
    W0 = np.asarray(inputs["L0_ffn_W0"], dtype=np.float32)
    bias0 = np.asarray(inputs["L0_ffn_bias0"], dtype=np.float32)
    W1 = np.asarray(inputs["L0_ffn_W1"], dtype=np.float32)
    bias1 = np.asarray(inputs["L0_ffn_bias1"], dtype=np.float32)

    xT = np.ascontiguousarray(x.transpose(0, 2, 1))

    def att_w(w):
        return np.ascontiguousarray(
            w.reshape(16, 128, 16, 128).transpose(2, 1, 0, 3).reshape(16, 128, 2048))

    wq4 = att_w(wq); wk4 = att_w(wk); wv4 = att_w(wv); wo4 = att_w(wo)
    w0t = np.ascontiguousarray(
        W0.reshape(64, 128, 16, 128).transpose(0, 3, 2, 1).reshape(64, 128, 2048))
    W1r = W1.reshape(16, 128, 64, 128).transpose(0, 2, 3, 1)
    W1r = W1r.reshape(16, 4, 16, 128, 128).transpose(0, 1, 3, 2, 4)
    w1t = np.ascontiguousarray(W1r.reshape(16, 4, 128, 2048))

    bvec = lambda v_: np.ascontiguousarray(v_.reshape(-1, 128).T)
    bq_t = bvec(bq); bk_t = bvec(bk); bv_t = bvec(bv); bo_t = bvec(bo)
    b0_t = bvec(bias0); b1_t = bvec(bias1)

    in_maps = []
    for c in range(C):
        a0 = XA0[c]
        hi = min(a0 + XAW, S)
        xa_c = np.zeros((B, H, XAW), dtype=np.float32)
        xa_c[:, :, 0:hi - a0] = xT[:, :, a0:hi]
        xf_c = np.ascontiguousarray(xT[:, :, 1024 * c:1024 * (c + 1)])
        mask_c = np.zeros((1, XAW), dtype=BFDT)
        mask_c[0, 0:hi - a0] = 1.0
        w0f_c = np.ascontiguousarray(w0t[8 * c:8 * (c + 1)])
        w1f_c = np.ascontiguousarray(w1t[2 * c:2 * (c + 1)].reshape(8, 128, 2048))
        in_maps.append({
            "xa": xa_c, "xf": xf_c,
            "wq4": wq4, "wk4": wk4, "wv4": wv4, "wo4": wo4,
            "w0f": w0f_c, "w1f": w1f_c,
            "bq": bq_t, "bk": bk_t, "bv": bv_t, "bo": bo_t,
            "b0": b0_t, "b1": b1_t,
            "cid": np.array([[c]], dtype=np.int32),
            "mask": mask_c,
        })
    return in_maps


def kernel(**inputs):
    in_maps = _prep_inputs(inputs)
    nc = build_program()
    res = run_bass_kernel_spmd(nc, in_maps, core_ids=list(range(C)))
    parts = [r["out"] for r in res.results]
    full = np.concatenate(parts, axis=2)
    return np.ascontiguousarray(full.transpose(0, 2, 1))


# version 32
# speedup vs baseline: 1.0507x; 1.0507x over previous
import math
import numpy as np
import ml_dtypes
import concourse.bass as bass
import concourse.mybir as mybir
from concourse.bass_utils import run_bass_kernel_spmd
from contextlib import ExitStack

F32 = mybir.dt.float32
BF16 = mybir.dt.bfloat16
I32 = mybir.dt.int32
AF = mybir.ActivationFunctionType
ALU = mybir.AluOpType
AX = mybir.AxisListType
BFDT = ml_dtypes.bfloat16

C = 8
S = 8192
B = 2
H = 2048
NH = 16
OWN = 1024
XAW = 1030
NQ = 103
WR = 1030
T = 512
XYP = 524
EPS = 1e-5
ISQ = 1.0 / math.sqrt(128.0)

M_LO = [(1024 * c + 9) // 10 for c in range(C)]
XA0 = [10 * m for m in M_LO]
DELTA = [XA0[c] - 1024 * c for c in range(C)]
VL = [(XA0[c + 1] if c + 1 < C else 8200) - XA0[c] for c in range(C)]
GCH = [(0, 50), (50, 50), (100, 3)]
CB = [(0, 512), (512, 512), (1024, 512), (1536, 512), (2048, 12)]


def _asm_const(c, t0):
    d = DELTA[c]
    m0 = 0 if t0 == 0 else (t0 - d + 9) // 10
    o0 = 10 * m0 + d - t0
    return m0, o0


class Plan:
    def __init__(self, nc, stack):
        self.nc = nc
        self.q = {"tensor": [], "vector": [], "scalar": [], "sync": [], "gpsimd": []}
        self.sem = {}
        self.cnt = {}
        self.stack = stack
        self.log = {}

    def mksem(self, name):
        self.sem[name] = self.stack.enter_context(self.nc.semaphore(name))
        self.cnt[name] = 0

    def wait(self, eng, name, val):
        if val is None or val <= 0:
            return
        h = self.sem[name]
        self.q[eng].append(lambda e, h=h, v=val: e.wait_ge(h, v))
        self.log.setdefault(eng, []).append(("wait", name, val))

    def group(self, eng, body, inc=None, k=1, tag=""):
        val = None
        if inc is not None:
            self.cnt[inc] += k
            val = self.cnt[inc]
            h = self.sem[inc]

            def run(e, body=body, h=h, k=k):
                body(e).then_inc(h, k)
        else:
            def run(e, body=body):
                body(e)
        self.q[eng].append(run)
        self.log.setdefault(eng, []).append(("inc", inc, k, tag))
        return val

    def dma(self, eng, out, in_, semname):
        return self.group(eng, lambda e, o=out, i=in_: e.dma_start(out=o, in_=i),
                          inc=semname, k=16)


def build_program(taps=()):
    nc = bass.Bass("TRN2", target_bir_lowering=False)

    xa = nc.dram_tensor("xa", [B, H, XAW], F32, kind="ExternalInput").ap()
    xf = nc.dram_tensor("xf", [B, H, OWN], F32, kind="ExternalInput").ap()
    wq4 = nc.dram_tensor("wq4", [NH, 128, H], F32, kind="ExternalInput").ap()
    wk4 = nc.dram_tensor("wk4", [NH, 128, H], F32, kind="ExternalInput").ap()
    wv4 = nc.dram_tensor("wv4", [NH, 128, H], F32, kind="ExternalInput").ap()
    wo4 = nc.dram_tensor("wo4", [NH, 128, H], F32, kind="ExternalInput").ap()
    w0f = nc.dram_tensor("w0f", [8, 128, 2048], F32, kind="ExternalInput").ap()
    w1f = nc.dram_tensor("w1f", [8, 128, 2048], F32, kind="ExternalInput").ap()
    bq_d = nc.dram_tensor("bq", [128, 16], F32, kind="ExternalInput").ap()
    bk_d = nc.dram_tensor("bk", [128, 16], F32, kind="ExternalInput").ap()
    bv_d = nc.dram_tensor("bv", [128, 16], F32, kind="ExternalInput").ap()
    bo_d = nc.dram_tensor("bo", [128, 16], F32, kind="ExternalInput").ap()
    b0_d = nc.dram_tensor("b0", [128, 64], F32, kind="ExternalInput").ap()
    b1_d = nc.dram_tensor("b1", [128, 16], F32, kind="ExternalInput").ap()
    cid_d = nc.dram_tensor("cid", [1, 1], I32, kind="ExternalInput").ap()
    mask_d = nc.dram_tensor("mask", [1, XAW], BF16, kind="ExternalInput").ap()
    out_d = nc.dram_tensor("out", [B, H, OWN], F32, kind="ExternalOutput").ap()

    tap_d = {}
    for tname, tshape, tdt in taps:
        tap_d[tname] = nc.dram_tensor("tap_" + tname, tshape, tdt, kind="ExternalOutput").ap()

    agin = nc.dram_tensor("agin", [2048, 2048], BF16, kind="Internal").ap()
    watt = nc.dram_tensor("watt", [3, 16, 128, 2048], BF16, kind="Internal").ap()
    agout = nc.dram_tensor("agout", [16384, 2048], BF16, kind="Internal", addr_space="Shared").ap()
    vrin = nc.dram_tensor("vrin", [128, 32], F32, kind="Internal").ap()
    vrout = nc.dram_tensor("vrout", [128, 32], F32, kind="Internal").ap()

    VSUM_LD = 32

    with ExitStack() as stack:
        sb = lambda n_, s_, d_=F32: stack.enter_context(nc.sbuf_tensor(n_, s_, d_))
        P = Plan(nc, stack)

        bq_t = sb("bq_t", [128, 16]); bk_t = sb("bk_t", [128, 16])
        bv_t = sb("bv_t", [128, 16]); bo_t = sb("bo_t", [128, 16])
        b0_t = sb("b0_t", [128, 64]); b1_t = sb("b1_t", [128, 16])
        attn_sb = sb("attn_sb", [128, NH * 208])
        vsum = sb("vsum", [128, 32])
        ones_col_bf = sb("ones_col_bf", [128, 1], BF16)
        ones_row_f = sb("ones_row_f", [1, 128])
        eps_c = sb("eps_c", [1, 1])
        cid_s = sb("cid_s", [1, 1], I32)

        for s_ in ["dXA", "dWS0", "dWS1", "dSTG0", "dSTG1", "dSW0", "dSW1", "dWS2", "dWS3", "dBI", "dWC",
                   "dVR", "dXF0", "dXF1", "dOUT", "dFS0", "dFS1", "dFS2",
                   "sPE", "sVE", "sSE", "sCC", "dTAP"]:
            P.mksem(s_)

        pv = [stack.enter_context(nc.psum_tensor(f"pv{i}", [128, 512], F32)) for i in range(2)]
        pe = [stack.enter_context(nc.psum_tensor(f"pe{i}", [128, 512], F32)) for i in range(2)]
        pb = [stack.enter_context(nc.psum_tensor(f"pb{i}", [128, 512], F32)) for i in range(2)]
        pst = [stack.enter_context(nc.psum_tensor(f"pst{i}", [1, 512], F32)) for i in range(2)]
        free = {}

        def guard(engq, p):
            f = free.get(id(p))
            if f is not None:
                P.wait(engq, f[0], f[1])

        def release(p, semname, val):
            free[id(p)] = (semname, val)

        RH = {}

        att = ExitStack()
        sba = lambda n_, s_, d_=F32: att.enter_context(nc.sbuf_tensor(n_, s_, d_))
        ones_row_bf = sba("ones_row_bf", [1, 128], BF16)
        mask_s = sba("mask_s", [1, XAW], BF16)
        x_bf = sba("x_bf", [128, NH * 2 * XAW], BF16)
        v_bf = sba("v_bf", [128, NH * 2 * XAW], BF16)
        xs_bf = sba("xs_bf", [128, NH * 206], BF16)
        q_all = sba("q_all", [128, NH * 206], BF16)
        ctx_bf = sba("ctx_bf", [128, NH * 208], BF16)
        kh_bf = sba("kh_bf", [128, 2 * XAW], BF16)
        wsl = sba("wsl", [128, 3 * 2048], BF16)
        w_sb = sba("w_sb", [1, 4 * XAW], BF16)
        prod_bf = sba("prod_bf", [128, 2 * XAW], BF16)
        vw_f = sba("vw_f", [128, 512], BF16)
        winsum = sba("winsum", [128, 104])
        zall = sba("zall", [1, NH * 2 * 104])
        vpart = sba("vpart", [128, 32])

        def wslb(i):
            return wslb_slot(SLOT_OF[i])

        P.dma("gpsimd", cid_s[:], cid_d, "dXA")
        P.dma("gpsimd", mask_s[:], mask_d, "dXA")
        for fc in range(NH):
            P.dma("gpsimd", x_bf[:, fc * 2 * XAW: fc * 2 * XAW + XAW],
                  xa[0, fc * 128:(fc + 1) * 128, :], "dXA")
        XA_B0 = P.cnt["dXA"]
        XA_ALL = XA_B0 + 16 * NH
        for dst, src in [(bq_t, bq_d), (bk_t, bk_d), (bv_t, bv_d), (bo_t, bo_d),
                         (b0_t, b0_d), (b1_t, b1_d)]:
            P.dma("sync", dst[:], src, "dBI")
        P.wait("scalar", "dBI", P.cnt["dBI"])
        P.wait("vector", "dBI", P.cnt["dBI"])

        P.wait("vector", "dXA", 32)

        def prelim(v):
            v.memset(ones_col_bf[:], 1.0)
            v.memset(ones_row_bf[:], 1.0)
            v.memset(ones_row_f[:], 1.0)
            v.memset(eps_c[:], EPS)
            idx = v.value_load(cid_s[:])
            RH["cid"] = v.compute_val(idx)
            return v.memset(zall[:, 0:104], 0.0)
        P.group("vector", prelim, inc="sVE")

        P.wait("vector", "dXA", XA_ALL)

        def build_xs(v):
            last = None
            for fc in range(NH):
                for b in range(B):
                    last = v.tensor_copy(
                        out=xs_bf[:, fc * 206 + b * NQ: fc * 206 + b * NQ + NQ],
                        in_=x_bf[:, fc * 2 * XAW + b * XAW: fc * 2 * XAW + b * XAW + WR: 10])
            return last
        xs_done = P.group("vector", build_xs, inc="sVE")

        vgrp = {}
        WSEQ = {"n": 0}
        wsl_cons = {}
        DWS = ["dWS0", "dWS1", "dWS2", "dWS3"]
        SLOT_OF = {}
        slot_prev = {}
        XS_FREE = {}

        def slot_of(si):
            if si < 32:
                return si % 3
            return [3, 0, 1, 2][(si - 32) % 4]

        def wslb_slot(sl):
            if sl == 3:
                return xs_bf[:, 0:2048]
            return wsl[:, sl * 2048: sl * 2048 + 2048]

        def wsl_load(queue, src):
            si = WSEQ["n"]; WSEQ["n"] += 1
            sl = slot_of(si)
            SLOT_OF[si] = sl
            prev = slot_prev.get(sl)
            if prev is not None:
                P.wait(queue, "sPE", wsl_cons[prev])
            if sl == 3 and prev is None:
                P.wait(queue, "sPE", XS_FREE["q"])
            slot_prev[sl] = si
            d = P.dma(queue, wslb_slot(sl), src, DWS[sl])
            return si, d

        wv_done = {}

        def load_wv(dc):
            wv_done[dc] = wsl_load("gpsimd", wv4[dc])

        load_wv(0)
        load_wv(1)
        for fc in range(NH):
            P.dma("gpsimd", x_bf[:, fc * 2 * XAW + XAW: fc * 2 * XAW + 2 * XAW],
                  xa[1, fc * 128:(fc + 1) * 128, :], "dXA")
        assert P.cnt["dXA"] == XA_ALL
        WC_DONE = {}
        WCD = {}
        WC_PIECE = {}
        for i in range(2):
            for pc in range(4):
                WC_PIECE[(2 + i * 4 + pc, )] = WC_PIECE.get((2 + i * 4 + pc,), []) + [(i, pc)]

        agv = agin.rearrange("(p s) c -> p s c", s=16)

        def agcast(sl):
            src = w0f[sl] if sl < 8 else w1f[sl - 8]
            P.dma("gpsimd", agv[:, sl, :], src, "dSW0")

        for dc in range(NH):
            vsi, vd = wv_done[dc]
            P.wait("tensor", DWS[SLOT_OF[vsi]], vd)
            for cb, (c0, cw) in enumerate(CB):
                if dc == 0:
                    P.wait("tensor", "dXA", XA_B0 if cb < 2 else XA_ALL)
                p = pv[(dc * 5 + cb) % 2]
                guard("tensor", p)

                def vmm(t, vsi=vsi, c0=c0, cw=cw, p=p):
                    last = None
                    for fc in range(NH):
                        last = t.matmul(p[:, 0:cw], wslb(vsi)[:, fc * 128:(fc + 1) * 128],
                                        x_bf[:, fc * 2 * XAW + c0: fc * 2 * XAW + c0 + cw],
                                        start=(fc == 0), stop=(fc == NH - 1))
                    return last
                g = P.group("tensor", vmm, inc="sPE")
                vgrp[(dc, cb)] = g
                P.wait("scalar", "sPE", g)
                ev = P.group("scalar",
                             lambda s_, dc=dc, c0=c0, cw=cw, p=p: s_.activation(
                                 out=v_bf[:, dc * 2 * XAW + c0: dc * 2 * XAW + c0 + cw],
                                 in_=p[:, 0:cw], func=AF.Identity, bias=bv_t[:, dc:dc + 1], scale=1.0),
                             inc="sSE")
                release(p, "sSE", ev)
            wsl_cons[wv_done[dc][0]] = P.cnt["sPE"]
            if dc + 2 < NH:
                load_wv(dc + 2)
                agcast(dc)
            for (i, pc) in WC_PIECE.get((dc,), []):
                P.wait("gpsimd", "sPE", vgrp[(dc, 4)])
                WC_DONE[i] = P.dma("gpsimd", watt[i, pc * 4:(pc + 1) * 4],
                                   [wq4, wk4][i][pc * 4:(pc + 1) * 4], "dWC")
                WCD[(i, pc)] = WC_DONE[i]
        V_EVAC_ALL = P.cnt["sSE"]
        agcast(14)
        agcast(15)
        P.wait("gpsimd", "dSW0", P.cnt["dSW0"])
        ag_done = P.group("gpsimd", lambda g: g.collective_compute(
            "AllGather", ALU.bypass, replica_groups=[list(range(C))],
            ins=[agin.opt()], outs=[agout.opt()]), inc="sCC")

        if "v_bf" in tap_d:
            P.wait("sync", "sSE", V_EVAC_ALL)
            P.dma("sync", tap_d["v_bf"], v_bf[:], "dTAP")

        qgrp = {}
        wq_done = {}
        for dc in range(2):
            if dc == 0:
                P.wait("sync", "dWC", WC_DONE[0])
            wq_done[dc] = wsl_load("sync", watt[0, dc])
        for dc in range(NH):
            qsi, qd = wq_done[dc]
            P.wait("tensor", DWS[SLOT_OF[qsi]], qd)
            if dc == 0:
                P.wait("tensor", "sVE", xs_done)
            p = pv[dc % 2]
            guard("tensor", p)

            def qmm(t, qsi=qsi, p=p):
                last = None
                for fc in range(NH):
                    last = t.matmul(p[:, 0:206], wslb(qsi)[:, fc * 128:(fc + 1) * 128],
                                    xs_bf[:, fc * 206:(fc + 1) * 206],
                                    start=(fc == 0), stop=(fc == NH - 1))
                return last
            g = P.group("tensor", qmm, inc="sPE")
            qgrp[dc] = g
            wsl_cons[qsi] = g
            if dc + 2 < NH:
                wq_done[dc + 2] = wsl_load("sync", watt[0, dc + 2])
            if dc == NH - 1:
                XS_FREE["q"] = qgrp[NH - 1]
            P.wait("scalar", "sPE", g)
            ev = P.group("scalar", lambda s_, dc=dc, p=p: s_.activation(
                out=q_all[:, dc * 206:(dc + 1) * 206], in_=p[:, 0:206],
                func=AF.Identity, bias=bq_t[:, dc:dc + 1], scale=1.0), inc="sSE")
            release(p, "sSE", ev)

        P.wait("vector", "sSE", V_EVAC_ALL)

        def vsum_switch(v):
            for ci in nc.Switch(engines=[nc.vector], index=[RH["cid"]], n=C):
                ln = VL[ci]
                for dc in range(NH):
                    for b in range(B):
                        v.tensor_reduce(
                            out=vpart[:, dc * 2 + b: dc * 2 + b + 1],
                            in_=v_bf[:, dc * 2 * XAW + b * XAW: dc * 2 * XAW + b * XAW + ln],
                            axis=AX.X, op=ALU.add)
            return v.scalar_tensor_tensor(
                out=vpart[:].rearrange("p (h b) -> p h b", b=2),
                in0=bv_t[:].unsqueeze(2).broadcast_to([128, 16, 2]),
                scalar=-1.0,
                in1=vpart[:].rearrange("p (h b) -> p h b", b=2),
                op0=ALU.mult, op1=ALU.add)
        vpart_done = P.group("vector", vsum_switch, inc="sVE")

        kgrp = {}
        st_prod = {}
        st_scores = {}
        st_wchain = {}
        st_wbp = {}
        st_ctx = {}

        wk_done = {}

        def load_wk(h):
            if h == 0:
                P.wait("sync", "dWC", WC_DONE[1])
            wk_done[h] = wsl_load("sync", watt[1, h])

        def emit_k(h, cbs=None):
            ksi, kd = wk_done[h]
            P.wait("tensor", DWS[SLOT_OF[ksi]], kd)
            if h >= 1 and (cbs is None or cbs[0] == 0):
                P.wait("scalar", "sVE", st_prod[h - 1])
            for cb, (c0, cw) in enumerate(CB):
                if cbs is not None and cb not in cbs:
                    continue
                p = pv[cb % 2]
                guard("tensor", p)

                def kmm(t, ksi=ksi, c0=c0, cw=cw, p=p):
                    last = None
                    for fc in range(NH):
                        last = t.matmul(p[:, 0:cw], wslb(ksi)[:, fc * 128:(fc + 1) * 128],
                                        x_bf[:, fc * 2 * XAW + c0: fc * 2 * XAW + c0 + cw],
                                        start=(fc == 0), stop=(fc == NH - 1))
                    return last
                g = P.group("tensor", kmm, inc="sPE")
                P.wait("scalar", "sPE", g)
                ev = P.group("scalar", lambda s_, c0=c0, cw=cw, p=p, h=h: s_.activation(
                    out=kh_bf[:, c0:c0 + cw], in_=p[:, 0:cw], func=AF.Identity,
                    bias=bk_t[:, h:h + 1], scale=1.0), inc="sSE")
                release(p, "sSE", ev)
            kgrp[h] = P.cnt["sPE"]
            wsl_cons[wk_done[h][0]] = P.cnt["sPE"]
            return P.cnt["sSE"]

        def emit_prod(h, kev):
            P.wait("vector", "sSE", kev)
            if h >= 1:
                P.wait("vector", "sPE", st_scores[h - 1])

            def pmul(v, h=h):
                kv = kh_bf[:].rearrange("p (m t) -> p m t", t=10)
                qv = q_all[:, h * 206:(h + 1) * 206].unsqueeze(2).broadcast_to([128, 206, 10])
                ov = prod_bf[:].rearrange("p (m t) -> p m t", t=10)
                return v.tensor_tensor(out=ov, in0=kv, in1=qv, op=ALU.mult)
            st_prod[h] = P.group("vector", pmul, inc="sVE")

        def wslot(h, b):
            j = (2 * h + b) % 4
            return w_sb[0:1, j * XAW: j * XAW + XAW]

        def emit_scores(h):
            for b in range(B):
                wr = wslot(h, b)
                for cix, (g0, ng) in enumerate(GCH):
                    par = (b * 3 + cix) % 2
                    w = 10 * ng
                    p = pst[par]
                    if cix == 0 and b == 0:
                        P.wait("tensor", "sVE", st_prod[h])
                    guard("tensor", p)
                    g = P.group("tensor", lambda t, b=b, g0=g0, w=w, p=p: t.matmul(
                        p[:, 0:w], ones_col_bf[:],
                        prod_bf[:, b * XAW + 10 * g0: b * XAW + 10 * g0 + w],
                        start=True, stop=True), inc="sPE")
                    st_scores[h] = g
                    P.wait("scalar", "sPE", g)
                    if cix == 0 and h >= 2:
                        P.wait("scalar", "sPE", st_wbp.get((h - 2, b), 0))
                    ev = P.group("scalar", lambda s_, g0=g0, w=w, p=p, wr=wr: s_.activation(
                        out=wr[0:1, 10 * g0: 10 * g0 + w],
                        in_=p[:, 0:w], func=AF.Exp, scale=ISQ), inc="sSE")
                    release(p, "sSE", ev)
                exp_done = P.cnt["sSE"]
                P.wait("vector", "sSE", exp_done)
                zsl = zall[0:1, (h * 2 + b) * 104: (h * 2 + b) * 104 + NQ]

                def wchain(v, wr=wr, zsl=zsl):
                    v.scalar_tensor_tensor(out=wr[0:1, 0:WR], in0=wr[0:1, 0:WR],
                                           scalar=-1.0, in1=mask_s[:, 0:WR],
                                           op0=ALU.add, op1=ALU.mult)
                    v.tensor_reduce(out=zsl,
                                    in_=wr[0:1, 0:WR].rearrange("p (m t) -> p m t", t=10),
                                    axis=AX.X, op=ALU.add)
                    v.tensor_scalar(out=zsl, in0=zsl,
                                    scalar1=float(S), scalar2=None, op0=ALU.add)
                    return v.reciprocal(out=zsl, in_=zsl)
                st_wchain[(h, b)] = P.group("vector", wchain, inc="sVE")

        def emit_wb(h, b, wrow_src):
            for cix, (g0, ng) in enumerate(GCH):
                par = (b * 3 + cix) % 2
                w = 10 * ng
                p = pb[par]
                P.wait("tensor", "sVE", st_wchain[(h, b)])
                guard("tensor", p)
                g = P.group("tensor", lambda t, g0=g0, w=w, p=p, wrow_src=wrow_src: t.matmul(
                    p[:, 0:w], ones_row_bf[:], wrow_src[0:1, 10 * g0: 10 * g0 + w],
                    start=True, stop=True), inc="sPE")
                P.wait("vector", "sPE", g)

                def vwin(v, h=h, b=b, g0=g0, ng=ng, w=w, p=p):
                    v.tensor_tensor(out=vw_f[:, 0:w],
                                    in0=v_bf[:, h * 2 * XAW + b * XAW + 10 * g0:
                                             h * 2 * XAW + b * XAW + 10 * g0 + w],
                                    in1=p[:, 0:w], op=ALU.mult)
                    return v.tensor_reduce(out=winsum[:, g0:g0 + ng],
                                           in_=vw_f[:, 0:w].rearrange("p (m t) -> p m t", t=10),
                                           axis=AX.X, op=ALU.add)
                vv = P.group("vector", vwin, inc="sVE")
                release(p, "sVE", vv)
            zsl = zall[0:1, (h * 2 + b) * 104: (h * 2 + b) * 104 + NQ]
            prz = pb[(b * 3 + 3) % 2]
            guard("tensor", prz)
            g = P.group("tensor", lambda t, prz=prz, zsl=zsl: t.matmul(
                prz[:, 0:NQ], ones_row_f[:], zsl, start=True, stop=True), inc="sPE")
            st_wbp[(h, b)] = g
            P.wait("vector", "sPE", g)

            def ctxop(v, h=h, b=b, prz=prz):
                return v.tensor_tensor(out=ctx_bf[:, h * 208 + b * NQ: h * 208 + (b + 1) * NQ],
                                       in0=winsum[:, 0:NQ], in1=prz[:, 0:NQ], op=ALU.mult)
            cv = P.group("vector", ctxop, inc="sVE")
            release(prz, "sVE", cv)
            st_ctx[h] = cv

        load_wk(0)
        load_wk(1)
        P.wait("gpsimd", "sVE", vpart_done)
        P.dma("gpsimd", vrin, vpart[:], "dVR")
        P.wait("gpsimd", "dVR", 16)
        ar_done = P.group("gpsimd", lambda g: g.collective_compute(
            "AllReduce", ALU.add, replica_groups=[list(range(C))],
            ins=[vrin.opt()], outs=[vrout.opt()]), inc="sCC")
        P.wait("gpsimd", "sCC", ar_done)
        vsum_ld = P.dma("gpsimd", vsum[:], vrout, "dVR")
        assert vsum_ld == VSUM_LD, vsum_ld

        for h in range(NH + 2):
            if h < NH:
                emit_k(h, cbs=(0, 1))
            if h >= 2:
                emit_wb(h - 2, 0, wslot(h - 2, 0))
                emit_wb(h - 2, 1, wslot(h - 2, 1))
            if h < NH:
                kev = emit_k(h, cbs=(2, 3, 4))
            if 1 <= h <= NH:
                emit_scores(h - 1)
            if h < NH:
                emit_prod(h, kev)
            if h + 2 < NH:
                load_wk(h + 2)

        XF0_PE = max(kgrp[NH - 1], st_wbp[(NH - 1, 0)], st_wbp[(NH - 1, 1)])
        XF0_VE = st_wchain[(NH - 1, 1)]

        P.wait("tensor", "sVE", st_ctx[NH - 1])
        P.wait("vector", "dVR", VSUM_LD)
        for h in range(NH):
            for b in range(B):
                zsl = zall[0:1, (h * 2 + b) * 104: (h * 2 + b) * 104 + NQ]
                prz = pb[(2 * h + b) % 2]
                guard("tensor", prz)
                g = P.group("tensor", lambda t, prz=prz, zsl=zsl: t.matmul(
                    prz[:, 0:NQ], ones_row_f[:], zsl, start=True, stop=True), inc="sPE")
                P.wait("vector", "sPE", g)

                def ctxadd(v, h=h, b=b, prz=prz):
                    return v.scalar_tensor_tensor(
                        out=ctx_bf[:, h * 208 + b * NQ: h * 208 + (b + 1) * NQ],
                        in0=prz[:, 0:NQ],
                        scalar=vsum[:, h * 2 + b: h * 2 + b + 1],
                        in1=ctx_bf[:, h * 208 + b * NQ: h * 208 + (b + 1) * NQ],
                        op0=ALU.mult, op1=ALU.add)
                cv = P.group("vector", ctxadd, inc="sVE")
                release(prz, "sVE", cv)
        ctx_final = P.cnt["sVE"]

        P.wait("vector", "dVR", VSUM_LD)
        ctxu_done = P.group("vector", lambda v: v.tensor_scalar(
            out=ctx_bf[:].rearrange("p (h c) -> p h c", c=208)[:, :, 206:208],
            in0=vsum[:].rearrange("p (h b) -> p h b", b=2),
            scalar1=1.0 / S, scalar2=None, op0=ALU.mult), inc="sVE")

        ogrp = {}
        wo_done = {}
        for dc in range(2):
            wo_done[dc] = wsl_load("gpsimd", wo4[dc])
        for dc in range(NH):
            osi, od = wo_done[dc]
            P.wait("tensor", DWS[SLOT_OF[osi]], od)
            if dc == 0:
                P.wait("tensor", "sVE", ctx_final)
                P.wait("tensor", "sVE", ctxu_done)
            p = pv[dc % 2]
            guard("tensor", p)

            def omm(t, osi=osi, p=p):
                last = None
                for hc in range(NH):
                    last = t.matmul(p[:, 0:208], wslb(osi)[:, hc * 128:(hc + 1) * 128],
                                    ctx_bf[:, hc * 208:(hc + 1) * 208],
                                    start=(hc == 0), stop=(hc == NH - 1))
                return last
            g = P.group("tensor", omm, inc="sPE")
            ogrp[dc] = g
            wsl_cons[osi] = g
            if dc + 2 < NH:
                wo_done[dc + 2] = wsl_load("gpsimd", wo4[dc + 2])
            P.wait("scalar", "sPE", g)
            ev = P.group("scalar", lambda s_, dc=dc, p=p: s_.activation(
                out=attn_sb[:, dc * 208:(dc + 1) * 208], in_=p[:, 0:208],
                func=AF.Identity, bias=bo_t[:, dc:dc + 1], scale=1.0), inc="sSE")
            release(p, "sSE", ev)
        OP_EVAC_ALL = P.cnt["sSE"]

        P.wait("vector", "sSE", OP_EVAC_ALL)

        def attnprime(v):
            last = None
            for dc in range(NH):
                for b in range(B):
                    last = v.tensor_scalar(
                        out=attn_sb[:, dc * 208 + b * NQ: dc * 208 + (b + 1) * NQ],
                        in0=attn_sb[:, dc * 208 + b * NQ: dc * 208 + (b + 1) * NQ],
                        scalar1=attn_sb[:, dc * 208 + 206 + b: dc * 208 + 207 + b],
                        scalar2=None, op0=ALU.subtract)
            return last
        attnp_done = P.group("vector", attnprime, inc="sVE")

        if "attn_sb" in tap_d:
            P.wait("sync", "sVE", attnp_done)
            P.dma("sync", tap_d["attn_sb"], attn_sb[:], "dTAP")
        if "ctx_bf" in tap_d:
            P.wait("sync", "sVE", st_ctx[NH - 1])
            P.wait("sync", "sVE", ctxu_done)
            P.dma("sync", tap_d["ctx_bf"], ctx_bf[:], "dTAP")
        if "vsum" in tap_d:
            P.wait("sync", "dVR", VSUM_LD)
            P.dma("sync", tap_d["vsum"], vsum[:], "dTAP")
        if "q_all" in tap_d:
            P.wait("sync", "sVE", attnp_done)
            P.dma("sync", tap_d["q_all"], q_all[:], "dTAP")

        ATT_PE = P.cnt["sPE"]
        ATT_VE = P.cnt["sVE"]
        ATT_SE = P.cnt["sSE"]
        att.close()

        ffn = ExitStack()
        sbf = lambda n_, s_, d_=F32: ffn.enter_context(nc.sbuf_tensor(n_, s_, d_))
        xy = [sbf(f"xy{i}", [128, NH * XYP]) for i in range(2)]
        ybf = [sbf(f"ybf{i}", [128, NH * T], BF16) for i in range(2)]
        dbf = sbf("dbf", [128, 64 * T], BF16)
        fsl = sbf("fsl", [128, 3 * 4352], BF16)
        sqt = sbf("sqt", [128, 2 * 512], BF16)
        mrow = sbf("mrow", [1, 512])
        vrow = sbf("vrow", [1, 512])

        def fslb(i):
            return fsl[:, (i % 3) * 4352:(i % 3) * 4352 + 4096]

        def agpair(r, sl):
            blk = agout[2048 * r: 2048 * (r + 1), :].rearrange("(p s) c -> p s c", s=16)
            return blk[:, sl:sl + 2, :].rearrange("p s c -> p (s c)")

        FS = ["dFS0", "dFS1", "dFS2"]
        pair_consumed = {}
        gpair = [0]

        def ln_stats(p_ix, sq_guard):
            xyt = xy[p_ix]
            yb = ybf[p_ix]

            guard("tensor", pst[0])
            sq_marks = []
            cast_done = None
            for fc in range(NH):
                if fc >= 2:
                    P.wait("vector", "sPE", sq_marks[fc - 2])
                elif fc == 0 and sq_guard:
                    P.wait("vector", "sPE", sq_guard)

                def castsq(v, fc=fc):
                    v.tensor_copy(out=yb[:, fc * T:(fc + 1) * T],
                                  in_=xyt[:, fc * XYP: fc * XYP + T])
                    return v.tensor_tensor(out=sqt[:, (fc % 2) * 512:(fc % 2) * 512 + T],
                                           in0=yb[:, fc * T:(fc + 1) * T],
                                           in1=yb[:, fc * T:(fc + 1) * T], op=ALU.mult)
                sqv = P.group("vector", castsq, inc="sVE")
                P.wait("tensor", "sVE", sqv)
                if fc == 0:
                    guard("tensor", pst[1])

                def sm2(t, fc=fc):
                    t.matmul(pst[0][:, 0:T], ones_col_bf[:], yb[:, fc * T:(fc + 1) * T],
                             start=(fc == 0), stop=(fc == NH - 1))
                    return t.matmul(pst[1][:, 0:T], ones_col_bf[:],
                                    sqt[:, (fc % 2) * 512:(fc % 2) * 512 + T],
                                    start=(fc == 0), stop=(fc == NH - 1))
                sq_marks.append(P.group("tensor", sm2, inc="sPE"))
            cast_done = sq_marks[-1]
            sumsq_done = sq_marks[-1]
            P.wait("vector", "sPE", sumsq_done)

            def rows(v):
                v.tensor_scalar(out=mrow[:, 0:T], in0=pst[0][:, 0:T], scalar1=1.0 / H,
                                scalar2=None, op0=ALU.mult)
                v.tensor_scalar(out=vrow[:, 0:T], in0=pst[1][:, 0:T], scalar1=1.0 / H,
                                scalar2=None, op0=ALU.mult)
                v.tensor_tensor(out=pst[0][:, 0:T], in0=mrow[:, 0:T], in1=mrow[:, 0:T], op=ALU.mult)
                return v.tensor_tensor(out=vrow[:, 0:T], in0=vrow[:, 0:T], in1=pst[0][:, 0:T],
                                       op=ALU.subtract)
            rows_done = P.group("vector", rows, inc="sVE")
            release(pst[0], "sVE", rows_done)
            release(pst[1], "sVE", rows_done)
            P.wait("scalar", "sVE", rows_done)

            def rstd(s_):
                s_.activation(out=vrow[:, 0:T], in_=vrow[:, 0:T], func=AF.Ln, bias=eps_c[0:1, 0:1])
                return s_.activation(out=vrow[:, 0:T], in_=vrow[:, 0:T], func=AF.Exp, scale=-0.5)
            rstd_done = P.group("scalar", rstd, inc="sSE")

            def emit_bcasts():
                P.wait("tensor", "sVE", rows_done)
                P.wait("tensor", "sSE", rstd_done)
                guard("tensor", pb[0]); guard("tensor", pb[1])
                P.group("tensor", lambda t: t.matmul(pb[0][:, 0:T], ones_row_f[:],
                                                     mrow[0:1, 0:T], start=True, stop=True), inc="sPE")
                return P.group("tensor", lambda t: t.matmul(pb[1][:, 0:T], ones_row_f[:],
                                                            vrow[0:1, 0:T], start=True, stop=True), inc="sPE")
            return emit_bcasts, sumsq_done

        def ln_apply(p_ix, bt):
            xyt = xy[p_ix]
            P.wait("vector", "sPE", bt)
            marks = []
            for fc0 in range(0, NH, 4):

                def apply_(v, fc0=fc0):
                    last = None
                    for fc in range(fc0, fc0 + 4):
                        v.tensor_tensor(out=xyt[:, fc * XYP: fc * XYP + T],
                                        in0=xyt[:, fc * XYP: fc * XYP + T], in1=pb[0][:, 0:T],
                                        op=ALU.subtract)
                        last = v.tensor_tensor(out=xyt[:, fc * XYP: fc * XYP + T],
                                               in0=xyt[:, fc * XYP: fc * XYP + T], in1=pb[1][:, 0:T],
                                               op=ALU.mult)
                    return last
                marks.append(P.group("vector", apply_, inc="sVE"))
            release(pb[0], "sVE", marks[-1])
            release(pb[1], "sVE", marks[-1])
            return marks

        TILES = [(0, 0), (0, T), (1, 0), (1, T)]
        last_ssq = [0]
        XF_LD = {}

        P.wait("sync", "sVE", XF0_VE)
        P.wait("sync", "sPE", XF0_PE)
        for fc in range(NH):
            P.dma("sync", xy[0][:, fc * XYP: fc * XYP + T],
                  xf[0, fc * 128:(fc + 1) * 128, 0:T], "dXF0")
        XF_LD[0] = P.cnt["dXF0"]

        def prep(t_ix):
            b, t0 = TILES[t_ix]
            p_ix = t_ix % 2
            dxf = "dXF0" if p_ix == 0 else "dXF1"
            if t_ix not in XF_LD:
                if t_ix >= 2:
                    P.wait("gpsimd", "dOUT", 256 * (t_ix - 1))
                for fc in range(NH):
                    P.dma("gpsimd", xy[p_ix][:, fc * XYP: fc * XYP + T],
                          xf[b, fc * 128:(fc + 1) * 128, t0:t0 + T], dxf)
                XF_LD[t_ix] = P.cnt[dxf]
            P.wait("vector", dxf, XF_LD[t_ix])
            P.wait("vector", "sVE", attnp_done)

            def uadd(v, b=b, p_ix=p_ix):
                last = None
                for fc in range(NH):
                    last = v.tensor_scalar(
                        out=xy[p_ix][:, fc * XYP: fc * XYP + T],
                        in0=xy[p_ix][:, fc * XYP: fc * XYP + T],
                        scalar1=attn_sb[:, fc * 208 + 206 + b: fc * 208 + 207 + b],
                        scalar2=None, op0=ALU.add)
                return last
            P.group("vector", uadd, inc="sVE")

            def sw_asm(v, b=b, t0=t0, p_ix=p_ix):
                for ci in nc.Switch(engines=[nc.vector], index=[RH["cid"]], n=C):
                    m0, o0 = _asm_const(ci, t0)
                    for fc in range(NH):
                        v.tensor_tensor(
                            out=xy[p_ix][:, fc * XYP + o0: fc * XYP + o0 + 511: 10],
                            in0=xy[p_ix][:, fc * XYP + o0: fc * XYP + o0 + 511: 10],
                            in1=attn_sb[:, fc * 208 + b * NQ + m0: fc * 208 + b * NQ + m0 + 52],
                            op=ALU.add)
                return v.tensor_copy(out=xy[p_ix][:, 0:1], in_=xy[p_ix][:, 0:1])
            P.group("vector", sw_asm, inc="sVE")

        def ln1_block(t_ix):
            p_ix = t_ix % 2
            bcasts, ssq = ln_stats(p_ix, last_ssq[0])
            last_ssq[0] = ssq
            res = [None]

            def finish():
                bt = bcasts()
                ln_apply(p_ix, bt)

                def recast(v, p_ix=p_ix):
                    last = None
                    for fc in range(NH):
                        last = v.tensor_copy(out=ybf[p_ix][:, fc * T:(fc + 1) * T],
                                             in_=xy[p_ix][:, fc * XYP: fc * XYP + T])
                    return last
                res[0] = P.group("vector", recast, inc="sVE")
            return finish, res

        def ln2_block(t_ix):
            b, t0 = TILES[t_ix]
            p_ix = t_ix % 2
            bcasts, ssq = ln_stats(p_ix, last_ssq[0])
            last_ssq[0] = ssq
            aps = ln_apply(p_ix, bcasts())
            for i, fc0 in enumerate(range(0, NH, 4)):
                q = ("sync" if (t_ix == 3 and i % 2 == 1) else "gpsimd")
                P.wait(q, "sVE", aps[i])
                for fc in range(fc0, fc0 + 4):
                    P.dma(q, out_d[b, fc * 128:(fc + 1) * 128, t0:t0 + T],
                          xy[p_ix][:, fc * XYP: fc * XYP + T], "dOUT")

        def load_pair(r, sl):
            pi = gpair[0]; gpair[0] += 1
            fs = FS[pi % 3]
            if pi < 3:
                P.wait("sync", "sCC", ag_done)
                if pi == 0:
                    P.wait("sync", "sSE", ATT_SE)
                    P.wait("sync", "sVE", ATT_VE)
                    P.wait("sync", "sPE", ATT_PE)
            else:
                P.wait("sync", "sPE", pair_consumed[pi - 3])
            sd = P.dma("sync", fslb(pi), agpair(r, sl), fs)
            P.wait("tensor", fs, sd)
            return pi

        def f1(t_ix, y_ready, hook=None, hook2=None):
            p_ix = t_ix % 2
            P.wait("tensor", "sVE", y_ready)
            for pr in range(32):
                if pr == 4 and hook is not None:
                    hook()
                if pr == 16 and hook2 is not None:
                    hook2()
                r, sl = pr // 4, (pr % 4) * 2
                pi = load_pair(r, sl)
                for half in range(2):
                    ncg = 2 * pr + half
                    p = pv[ncg % 2]
                    guard("tensor", p)

                    def dmm(t, pi=pi, half=half, p=p, p_ix=p_ix):
                        last = None
                        for fc in range(NH):
                            last = t.matmul(
                                p[:, 0:T],
                                fslb(pi)[:, half * 2048 + fc * 128: half * 2048 + (fc + 1) * 128],
                                ybf[p_ix][:, fc * T:(fc + 1) * T],
                                start=(fc == 0), stop=(fc == NH - 1))
                        return last
                    g = P.group("tensor", dmm, inc="sPE")
                    pair_consumed[pi] = g
                    P.wait("scalar", "sPE", g)
                    ev = P.group("scalar", lambda s_, ncg=ncg, p=p: s_.activation(
                        out=dbf[:, ncg * T:(ncg + 1) * T], in_=p[:, 0:T], func=AF.Gelu,
                        bias=b0_t[:, ncg:ncg + 1], scale=1.0), inc="sSE")
                    release(p, "sSE", ev)

        def f2(t_ix, lnfin=None):
            p_ix = t_ix % 2
            ea_marks = []
            for hc in range(NH):
                if hc == 1 and lnfin is not None:
                    lnfin()
                pE = pe[hc % 2]
                guard("tensor", pE)
                g = None
                for subpair in range(2):
                    r, sl = hc // 2, 8 + (hc % 2) * 4 + 2 * subpair
                    pi = load_pair(r, sl)
                    for half in range(2):
                        sub = 2 * subpair + half

                        def emm(t, pi=pi, half=half, sub=sub, pE=pE):
                            last = None
                            for j in range(NH):
                                last = t.matmul(
                                    pE[:, 0:T],
                                    fslb(pi)[:, half * 2048 + j * 128: half * 2048 + (j + 1) * 128],
                                    dbf[:, (sub * 16 + j) * T:(sub * 16 + j + 1) * T],
                                    start=(sub == 0 and j == 0),
                                    stop=(sub == 3 and j == NH - 1))
                            return last
                        g = P.group("tensor", emm, inc="sPE")
                        pair_consumed[pi] = g
                P.wait("vector", "sPE", g)
                ea = P.group("vector", lambda v, hc=hc, p_ix=p_ix, pE=pE: v.scalar_tensor_tensor(
                    out=xy[p_ix][:, hc * XYP: hc * XYP + T],
                    in0=pE[:, 0:T], scalar=b1_t[:, hc:hc + 1],
                    in1=xy[p_ix][:, hc * XYP: hc * XYP + T],
                    op0=ALU.add, op1=ALU.add), inc="sVE")
                release(pE, "sVE", ea)
                ea_marks.append(ea)

        prep(0)
        fin, res = ln1_block(0)
        fin()
        y_ready = res[0]
        for t_ix in range(4):
            hook = (lambda t=t_ix: ln2_block(t - 1)) if t_ix >= 1 else None
            hook2 = (lambda t=t_ix: prep(t + 1)) if t_ix + 1 < 4 else None
            f1(t_ix, y_ready, hook=hook, hook2=hook2)
            lnfin = None
            if t_ix + 1 < 4:
                fin, res = ln1_block(t_ix + 1)
                lnfin = fin
            f2(t_ix, lnfin=lnfin)
            if t_ix + 1 < 4:
                y_ready = res[0]
        ln2_block(3)

        P.wait("sync", "dOUT", P.cnt["dOUT"])
        if tap_d:
            P.wait("sync", "dTAP", P.cnt["dTAP"])

        blk = stack.enter_context(nc.Block())

        @blk.gpsimd
        def _(g):
            for fn in P.q["gpsimd"]:
                fn(g)

        @blk.vector
        def _(v):
            for fn in P.q["vector"]:
                fn(v)

        @blk.scalar
        def _(s_):
            for fn in P.q["scalar"]:
                fn(s_)

        @blk.tensor
        def _(t):
            for fn in P.q["tensor"]:
                fn(t)

        @blk.sync
        def _(sy):
            for fn in P.q["sync"]:
                fn(sy)

        ffn.close()
    global _LAST_PLAN_LOG
    _LAST_PLAN_LOG = P.log
    return nc


_LAST_PLAN_LOG = None


def _prep_inputs(inputs):
    x = np.asarray(inputs["L0_attn_hidden_states"], dtype=np.float32)
    wq = np.asarray(inputs["L0_attn_wq"], dtype=np.float32)
    bq = np.asarray(inputs["L0_attn_bq"], dtype=np.float32)
    wk = np.asarray(inputs["L0_attn_wk"], dtype=np.float32)
    bk = np.asarray(inputs["L0_attn_bk"], dtype=np.float32)
    wv = np.asarray(inputs["L0_attn_wv"], dtype=np.float32)
    bv = np.asarray(inputs["L0_attn_bv"], dtype=np.float32)
    wo = np.asarray(inputs["L0_attn_wo"], dtype=np.float32)
    bo = np.asarray(inputs["L0_attn_bo"], dtype=np.float32)
    W0 = np.asarray(inputs["L0_ffn_W0"], dtype=np.float32)
    bias0 = np.asarray(inputs["L0_ffn_bias0"], dtype=np.float32)
    W1 = np.asarray(inputs["L0_ffn_W1"], dtype=np.float32)
    bias1 = np.asarray(inputs["L0_ffn_bias1"], dtype=np.float32)

    xT = np.ascontiguousarray(x.transpose(0, 2, 1))

    def att_w(w):
        return np.ascontiguousarray(
            w.reshape(16, 128, 16, 128).transpose(2, 1, 0, 3).reshape(16, 128, 2048))

    wq4 = att_w(wq); wk4 = att_w(wk); wv4 = att_w(wv); wo4 = att_w(wo)
    w0t = np.ascontiguousarray(
        W0.reshape(64, 128, 16, 128).transpose(0, 3, 2, 1).reshape(64, 128, 2048))
    W1r = W1.reshape(16, 128, 64, 128).transpose(0, 2, 3, 1)
    W1r = W1r.reshape(16, 4, 16, 128, 128).transpose(0, 1, 3, 2, 4)
    w1t = np.ascontiguousarray(W1r.reshape(16, 4, 128, 2048))

    bvec = lambda v_: np.ascontiguousarray(v_.reshape(-1, 128).T)
    bq_t = bvec(bq); bk_t = bvec(bk); bv_t = bvec(bv); bo_t = bvec(bo)
    b0_t = bvec(bias0); b1_t = bvec(bias1)

    in_maps = []
    for c in range(C):
        a0 = XA0[c]
        hi = min(a0 + XAW, S)
        xa_c = np.zeros((B, H, XAW), dtype=np.float32)
        xa_c[:, :, 0:hi - a0] = xT[:, :, a0:hi]
        xf_c = np.ascontiguousarray(xT[:, :, 1024 * c:1024 * (c + 1)])
        mask_c = np.zeros((1, XAW), dtype=BFDT)
        mask_c[0, 0:hi - a0] = 1.0
        w0f_c = np.ascontiguousarray(w0t[8 * c:8 * (c + 1)])
        w1f_c = np.ascontiguousarray(w1t[2 * c:2 * (c + 1)].reshape(8, 128, 2048))
        in_maps.append({
            "xa": xa_c, "xf": xf_c,
            "wq4": wq4, "wk4": wk4, "wv4": wv4, "wo4": wo4,
            "w0f": w0f_c, "w1f": w1f_c,
            "bq": bq_t, "bk": bk_t, "bv": bv_t, "bo": bo_t,
            "b0": b0_t, "b1": b1_t,
            "cid": np.array([[c]], dtype=np.int32),
            "mask": mask_c,
        })
    return in_maps


def kernel(**inputs):
    in_maps = _prep_inputs(inputs)
    nc = build_program()
    res = run_bass_kernel_spmd(nc, in_maps, core_ids=list(range(C)))
    parts = [r["out"] for r in res.results]
    full = np.concatenate(parts, axis=2)
    return np.ascontiguousarray(full.transpose(0, 2, 1))
